# version 15
# baseline (speedup 1.0000x reference)
import os
import numpy as np
import ml_dtypes
import concourse.bass as bass
import concourse.mybir as mybir
from concourse.bass_utils import run_bass_kernel_spmd
from contextlib import ExitStack

F32 = mybir.dt.float32
BF16 = mybir.dt.bfloat16
I32 = mybir.dt.int32
AF = mybir.ActivationFunctionType
OP = mybir.AluOpType

T = 4096
DM = 1024
DI = 2048
NST = 16
TC = 256
NCH = T // TC
NDIB = DI // 128
MAGIC = 12582912.0


class Emit:

    DMA_STREAMS = {"qa": ("q", 8), "qb": ("q", 6), "qc": ("q", 6), "gb": ("g", 6)}

    def __init__(self, nc, es):
        self.nc = nc
        self.eng = {"v": nc.vector, "s": nc.scalar, "t": nc.tensor,
                    "q": nc.sync, "g": nc.gpsimd}
        self.sem = {k: es.enter_context(nc.semaphore("sem_" + k)) for k in "vstg"}
        self.cnt = {k: 0 for k in "vstg"}
        for st, (_, r) in self.DMA_STREAMS.items():
            for i in range(r):
                k = f"{st}{i}"
                self.sem[k] = es.enter_context(nc.semaphore("sem_" + k))
                self.cnt[k] = 0
        self.dtotal = {st: 0 for st in self.DMA_STREAMS}
        self.track = {}
        self.excl = {"pA", "pB", "pC", "pD", "pE", "pF", "pT1", "pT2"}
        self.satisfied = {k: {} for k in self.eng}

    def _need(self, e, reads, writes):
        want = {}

        def add(tok):
            if tok is None:
                return
            s, v = tok
            if v > want.get(s, -1):
                want[s] = v
        for k in reads:
            w, rs = self.track.get(k, (None, []))
            add(w)
            if k in self.excl:
                for r in rs:
                    add(r)
        for k in writes:
            w, rs = self.track.get(k, (None, []))
            add(w)
            for r in rs:
                add(r)
        sat = self.satisfied[e]
        return {s: v for s, v in want.items() if v > sat.get(s, -1)}

    def _mark(self, e, need):
        sat = self.satisfied[e]
        for s, v in need.items():
            sat[s] = max(sat.get(s, -1), v)

    def __call__(self, e, make, reads=(), writes=()):
        need = self._need(e, reads, writes)
        items = sorted(need.items())
        for s, v in items[1:]:
            self.eng[e].wait_ge(self.sem[s], v)
        inst = make(self.eng[e])
        if items:
            inst._wait_ge(self.sem[items[0][0]], items[0][1])
        self._mark(e, need)
        inst.then_inc(self.sem[e], 1)
        self.cnt[e] += 1
        tok = (e, self.cnt[e])
        for k in reads:
            w, rs = self.track.get(k, (None, []))
            rs.append(tok)
            self.track[k] = (w, rs)
        for k in writes:
            self.track[k] = (tok, [])
        return tok

    def dma(self, stream, out, in_, reads=(), writes=()):
        qe, r = self.DMA_STREAMS[stream]
        sub = f"{stream}{self.dtotal[stream] % r}"
        self.dtotal[stream] += 1
        need = self._need(qe, reads, writes)
        if self.cnt[sub] > 0 and need.get(sub, -1) < self.cnt[sub]:
            need[sub] = self.cnt[sub]
        for s, v in sorted(need.items()):
            self.eng[qe].wait_ge(self.sem[s], v)
        inst = self.eng[qe].dma_start(out=out, in_=in_)
        self._mark(qe, need)
        inst.then_inc(self.sem[sub], 16)
        self.cnt[sub] += 16
        tok = (sub, self.cnt[sub])
        for k in reads:
            w, rs = self.track.get(k, (None, []))
            rs.append(tok)
            self.track[k] = (w, rs)
        for k in writes:
            self.track[k] = (tok, [])
        return tok

    def barrier_all(self):
        for e in self.eng:
            sat = self.satisfied[e]
            for s, c in self.cnt.items():
                if c > sat.get(s, -1):
                    self.eng[e].wait_ge(self.sem[s], c)
                    sat[s] = c


def rev_ap(handle, parts, width):
    return bass.AP(handle, width - 1, [[width, parts], [-1, width]])


def build_nc():
    _phases = os.environ.get("KMK_PHASES", "0MF")
    _nch = int(os.environ.get("KMK_CHUNKS", str(NCH)))
    _dirs = os.environ.get("KMK_DIRS", "fb")
    _stg = os.environ.get("KMK_STAGES", "LICXEY12DSO")
    nc = bass.Bass("TRN2", target_bir_lowering=False)
    x_h = nc.dram_tensor("x", [T, DM], F32, kind="ExternalInput").ap()
    inw_h = {d: nc.dram_tensor(f"inw_{d}", [128, 8 * 32 * 128], F32, kind="ExternalInput").ap()
             for d in "fb"}
    xpw_h = nc.dram_tensor("xpw", [128, 2 * NDIB * 96], F32, kind="ExternalInput").ap()
    dtw_h = nc.dram_tensor("dtw", [64, 2 * DI], F32, kind="ExternalInput").ap()
    outw_h = {d: nc.dram_tensor(f"outw_{d}", [128, 8 * DI], F32, kind="ExternalInput").ap()
              for d in "fb"}
    projt_h = nc.dram_tensor("projt", [128, 2 * 8 * DM], F32, kind="ExternalInput").ap()
    cw_h = nc.dram_tensor("cw", [128, 2 * NDIB * 4], F32, kind="ExternalInput").ap()
    cb_h = nc.dram_tensor("cb", [128, 2 * NDIB], F32, kind="ExternalInput").ap()
    dtb_h = nc.dram_tensor("dtb", [128, 2 * NDIB], F32, kind="ExternalInput").ap()
    alog_h = nc.dram_tensor("alog", [128, 2 * NDIB * NST], F32, kind="ExternalInput").ap()
    dd_h = nc.dram_tensor("dd", [128, 2 * NDIB], F32, kind="ExternalInput").ap()
    pb_h = nc.dram_tensor("pb", [128, 8], F32, kind="ExternalInput").ap()
    tern_h = nc.dram_tensor("tern", [128, 8 * DM], I32, kind="ExternalInput").ap()
    fbias_h = nc.dram_tensor("fbias", [1, DM], F32, kind="ExternalInput").ap()
    ascale_h = nc.dram_tensor("ascale", [1, 1], F32, kind="ExternalInput").ap()
    ident_h = nc.dram_tensor("ident", [128, 128], BF16, kind="ExternalInput").ap()

    out_h = nc.dram_tensor("out", [T, DM], F32, kind="ExternalOutput").ap()

    inwbf_d = {d: nc.dram_tensor(f"inwbf_{d}", [128, 8 * 32 * 128], BF16, kind="Internal").ap()
               for d in "fb"}
    wft_d = {d: nc.dram_tensor(f"wft_{d}", [128, 8 * NDIB * 128], BF16, kind="Internal").ap()
             for d in "fb"}
    attn_d = nc.dram_tensor("attn", [DM, T], BF16, kind="Internal").ap()
    x2_d = nc.dram_tensor("x2", [T, DM], F32, kind="Internal").ap()

    with ExitStack() as es:
        E = Emit(nc, es)
        pA = es.enter_context(nc.psum_tensor("pA", [128, 512], F32))
        pB = es.enter_context(nc.psum_tensor("pB", [128, 512], F32))
        pC = es.enter_context(nc.psum_tensor("pC", [128, 512], F32))
        pD = es.enter_context(nc.psum_tensor("pD", [128, 512], F32))
        pE = es.enter_context(nc.psum_tensor("pE", [128, 512], F32))
        pF = es.enter_context(nc.psum_tensor("pF", [128, 512], F32))
        pT1 = es.enter_context(nc.psum_tensor("pT1", [128, 1024], BF16))
        pT2 = es.enter_context(nc.psum_tensor("pT2", [128, 1024], BF16))

        id_sb = es.enter_context(nc.sbuf_tensor("id_sb", [128, 128], BF16))
        cw_sb = es.enter_context(nc.sbuf_tensor("cw_sb", [128, 2 * NDIB * 4], F32))
        cb_sb = es.enter_context(nc.sbuf_tensor("cb_sb", [128, 2 * NDIB], F32))
        dtb_sb = es.enter_context(nc.sbuf_tensor("dtb_sb", [128, 2 * NDIB], F32))
        a_sb = es.enter_context(nc.sbuf_tensor("a_sb", [128, 2 * NDIB * NST], F32))
        dd_sb = es.enter_context(nc.sbuf_tensor("dd_sb", [128, 2 * NDIB], F32))
        pb_sb = es.enter_context(nc.sbuf_tensor("pb_sb", [128, 8], F32))
        xpw_sb = es.enter_context(nc.sbuf_tensor("xpw_sb", [128, 2 * NDIB * 96], BF16))
        dtw_sb = es.enter_context(nc.sbuf_tensor("dtw_sb", [64, 2 * DI], BF16))
        st_sb = es.enter_context(nc.sbuf_tensor("st_sb", [128, NDIB * NST], F32))
        ko_sb = es.enter_context(nc.sbuf_tensor("ko_sb", [128, 4], F32))
        ones_bf = es.enter_context(nc.sbuf_tensor("ones_bf", [1, 128], BF16))
        inw_b = [es.enter_context(nc.sbuf_tensor(f"inw_b{i}", [128, 1024], BF16))
                 for i in range(2)]
        wf_b = [es.enter_context(nc.sbuf_tensor(f"wf_b{i}", [128, 2048], BF16))
                for i in range(3)]

        with nc.sbuf_tensor("w_ld", [128, 4096], F32) as w_ld, \
             nc.sbuf_tensor("w_cst", [128, 4096], BF16) as w_cst, \
             nc.sbuf_tensor("projt_sb", [128, 8 * DM], BF16) as projt_sb, \
             nc.sbuf_tensor("outw_sb", [128, 8 * DI], BF16) as outw_sb, \
             nc.sbuf_tensor("wfev", [128, 1024], BF16) as wfev:
            E.dma("qa", id_sb[:], ident_h, [], ["id_sb"])
            E("v", lambda v: v.memset(ko_sb[:, 0:1], 1e-5), [], ["ko"])
            E("v", lambda v: v.memset(ko_sb[:, 1:2], 1.0), ["ko"], ["ko"])
            E("v", lambda v: v.memset(ko_sb[:, 2:3], 1.0 / DM), ["ko"], ["ko"])
            E("v", lambda v: v.memset(ones_bf[:], 1.0), [], ["ones"])
            E.dma("qa", cw_sb[:], cw_h, [], ["cw_sb"])
            E.dma("qa", cb_sb[:], cb_h, [], ["cb_sb"])
            E.dma("qa", dtb_sb[:], dtb_h, [], ["dtb_sb"])
            E.dma("qa", dd_sb[:], dd_h, [], ["dd_sb"])
            E.dma("qa", pb_sb[:], pb_h, [], ["pb_sb"])
            E.dma("qa", a_sb[:], alog_h, [], ["a_sb"])
            E("s", lambda s: s.activation(out=a_sb[:], in_=a_sb[:], func=AF.Exp),
              ["a_sb"], ["a_sb"])
            E("v", lambda v: v.tensor_scalar_mul(a_sb[:], a_sb[:], -1.0),
              ["a_sb"], ["a_sb"])
            E.dma("qa", w_ld[:, 0:2 * NDIB * 96], xpw_h, [], ["w_ld"])
            E("v", lambda v: v.tensor_copy(out=xpw_sb[:], in_=w_ld[:, 0:2 * NDIB * 96]),
              ["w_ld"], ["xpw_sb"])
            E.dma("qa", w_ld[0:64, 0:2 * DI], dtw_h, [], ["w_ld"])
            E("v", lambda v: v.tensor_copy(out=dtw_sb[:], in_=w_ld[0:64, 0:2 * DI]),
              ["w_ld"], ["dtw_sb"])
            for d in "fb":
                for j in range(8):
                    sl = slice(j * 4096, (j + 1) * 4096)
                    E.dma("qa", w_ld[:], inw_h[d][:, sl], [], ["w_ld"])
                    E("v", lambda v: v.tensor_copy(out=w_cst[:], in_=w_ld[:]),
                      ["w_ld"], ["w_cst"])
                    E.dma("qa", inwbf_d[d][:, sl], w_cst[:], ["w_cst"], [])
            for di_, d in enumerate("fb"):
                for j in range(4):
                    sl = slice(j * 4096, (j + 1) * 4096)
                    E.dma("qa", w_ld[:], outw_h[d][:, sl], [], ["w_ld"])
                    E("v", lambda v, sl=sl: v.tensor_copy(out=outw_sb[:, sl], in_=w_ld[:]),
                      ["w_ld"], ["outw_sb"])
                for j in range(2):
                    src = projt_h[:, di_ * 8 * DM + j * 4096: di_ * 8 * DM + (j + 1) * 4096]
                    sl = slice(j * 4096, (j + 1) * 4096)
                    E.dma("qa", w_ld[:], src, [], ["w_ld"])
                    E("v", lambda v, sl=sl: v.tensor_copy(out=projt_sb[:, sl], in_=w_ld[:]),
                      ["w_ld"], ["projt_sb"])
                for dib in range(NDIB):
                    for half in range(2):
                        ps, pk = (pA, "pA") if half == 0 else (pB, "pB")
                        for kc in range(8):
                            E("t", lambda t, dib=dib, half=half, kc=kc, ps=ps:
                              t.matmul(out=ps[:, 0:512],
                                       lhsT=outw_sb[:, kc * DI + dib * 128: kc * DI + (dib + 1) * 128],
                                       rhs=projt_sb[:, kc * DM + half * 512: kc * DM + (half + 1) * 512],
                                       start=(kc == 0), stop=(kc == 7)),
                              ["outw_sb", "projt_sb"],
                              [pk] if kc in (0, 7) else [])
                        E("s", lambda s, half=half, ps=ps: s.copy(
                            out=wfev[:, half * 512:(half + 1) * 512], in_=ps[:, 0:512]),
                          [pk], ["wfev"])
                    for dmb in range(8):
                        E.dma("qa", wft_d[d][:, (dmb * NDIB + dib) * 128:(dmb * NDIB + dib + 1) * 128],
                              wfev[:, dmb * 128:(dmb + 1) * 128], ["wfev"], [])
            E.barrier_all()

        with ExitStack() as ms:
          if "M" in _phases:
            def sb(name, shape, dt_=F32):
                return ms.enter_context(nc.sbuf_tensor(name, shape, dt_))
            x_t = [[sb("x_t00", [128, DM]), sb("x_t01", [128, DM])],
                   [sb("x_t10", [128, DM]), sb("x_t11", [128, DM])]]
            ln_s = sb("ln_s", [128, DM])
            ln_c = sb("ln_c", [128, 8])
            xln_tm = sb("xln_tm", [128, DM], BF16)
            xln_fm = sb("xln_fm", [128, 8 * TC], BF16)
            xi_sb = sb("xi_sb", [128, NDIB * 262])
            szb_sb = [sb("szb0", [128, NDIB * TC], BF16), sb("szb1", [128, NDIB * TC], BF16)]
            xcb_sb = sb("xcb_sb", [128, NDIB * TC], BF16)
            ysz_sb = sb("ysz_sb", [128, NDIB * TC], BF16)
            xdbl = sb("xdbl", [96, TC])
            xdblb = sb("xdblb", [64, TC], BF16)
            bcsrc = sb("bcsrc", [32, TC], BF16)
            bcflat = sb("bcflat", [1, 32 * TC], BF16)
            Bb_sb = sb("Bb_sb", [128, NST * TC], BF16)
            Cb_sb = sb("Cb_sb", [128, NST * TC], BF16)
            NG = 8
            EXT = NG * (TC + 1)
            dAx = [sb("dAx0", [128, EXT]), sb("dAx1", [128, EXT])]
            uBx = sb("uBx", [128, EXT])
            _hx = sb("Hx0", [128, EXT])
            Hx = [_hx, _hx]
            hcn = sb("hcn", [128, NST * TC])
            dts = [sb("dts0", [128, TC]), sb("dts1", [128, TC])]
            us_s = sb("us_s", [128, TC])
            ys_s = [sb("ys_s0", [128, TC]), sb("ys_s1", [128, TC])]
            cacc = [sb("cacc0", [128, TC]), sb("cacc1", [128, TC])]
            sp_s = sb("sp_s", [128, TC])
            af_sb = sb("af_sb", [128, 8 * TC], BF16)
            aev = sb("aev", [128, TC], BF16)

            def frontend_pieces(d, d_i, fwd, ch, par):
                t0 = ch * TC
                pieces = []
                def ln_tile(tt):
                    rows = slice(t0 + tt * 128, t0 + tt * 128 + 128)
                    xk = f"x_t{par}{tt}"
                    E.dma("qb", x_t[par][tt][:], x_h[rows, :], [], [xk])
                    E("s", lambda s, tt=tt: s.activation(
                        out=ln_s[:], in_=x_t[par][tt][:], func=AF.Square,
                        accum_out=ln_c[:, 2:3]), [xk, "ln_s"], ["ln_s", "ln_c"])
                    E("v", lambda v, tt=tt: v.reduce_sum(
                        ln_c[:, 0:1], x_t[par][tt][:], axis=mybir.AxisListType.X),
                      [xk], ["ln_c"])
                    E("v", lambda v: v.tensor_scalar_mul(
                        ln_c[:, 1:2], ln_c[:, 0:1], -1.0 / DM), ["ln_c"], ["ln_c"])
                    E("v", lambda v: v.tensor_mul(
                        ln_c[:, 5:6], ln_c[:, 1:2], ln_c[:, 1:2]), ["ln_c"], ["ln_c"])
                    E("v", lambda v: v.tensor_scalar(
                        out=ln_c[:, 6:7], in0=ln_c[:, 5:6], scalar1=-1.0, scalar2=1e-5,
                        op0=OP.mult, op1=OP.add), ["ln_c"], ["ln_c"])
                    E("v", lambda v, tt=tt: v.tensor_scalar_add(
                        ln_s[:], x_t[par][tt][:], ln_c[:, 1:2]), [xk, "ln_c", "ln_s"], ["ln_s"])
                    E("s", lambda s: s.activation(
                        out=ln_c[:, 3:4], in_=ln_c[:, 2:3], func=AF.Sqrt,
                        scale=ko_sb[:, 2:3], bias=ln_c[:, 6:7]), ["ln_c", "ko"], ["ln_c"])
                    E("v", lambda v: v.reciprocal(out=ln_c[:, 4:5], in_=ln_c[:, 3:4]),
                      ["ln_c"], ["ln_c"])
                    E("v", lambda v: v.tensor_scalar_mul(
                        xln_tm[:], ln_s[:], ln_c[:, 4:5]), ["ln_s", "ln_c"], ["xln_tm"])
                    for dmb in range(8):
                        E("t", lambda t, dmb=dmb: t.transpose(
                            out=pT1[:, dmb * 128:(dmb + 1) * 128],
                            in_=xln_tm[:, dmb * 128:(dmb + 1) * 128],
                            identity=id_sb[:]), ["xln_tm", "id_sb"], ["pT1"])
                    for dmb in range(8):
                        E("s", lambda s, dmb=dmb, tt=tt: s.copy(
                            out=xln_fm[:, dmb * TC + tt * 128: dmb * TC + tt * 128 + 128],
                            in_=pT1[:, dmb * 128:(dmb + 1) * 128]), ["pT1"], ["xln_fm"])
                for tt in range(2):
                    pieces.append((lambda tt=tt: ln_tile(tt)))
                def ip_dib(dib):
                    ip = dib % 2
                    ib, ibk = inw_b[ip], f"inw{ip}"
                    E.dma("qa", ib[:], inwbf_d[d][:, dib * 1024:(dib + 1) * 1024], [], [ibk])
                    ps, pk = (pA, "pA") if ip == 0 else (pB, "pB")
                    for kc in range(8):
                        E("t", lambda t, kc=kc, ps=ps, ib=ib:
                          t.matmul(out=ps[:, 0:TC],
                                   lhsT=ib[:, kc * 128:(kc + 1) * 128],
                                   rhs=xln_fm[:, kc * TC:(kc + 1) * TC],
                                   start=(kc == 0), stop=(kc == 7)),
                          [ibk, "xln_fm"], [pk] if kc in (0, 7) else [])
                    if dib < NDIB:
                        E("s", lambda s, dib=dib, ps=ps: s.copy(
                            out=xi_sb[:, dib * 262 + 3: dib * 262 + 3 + TC],
                            in_=ps[:, 0:TC]), [pk], ["xi"])
                    else:
                        zb = dib - NDIB
                        E("s", lambda s, zb=zb, ps=ps: s.activation(
                            out=szb_sb[par][:, zb * TC:(zb + 1) * TC], in_=ps[:, 0:TC],
                            func=AF.Silu), [pk], [f"szb{par}"])
                for dib in range(32):
                    pieces.append((lambda dib=dib: ip_dib(dib)))
                return pieces

            def frontend(d, d_i, fwd, ch, par):
                for p in frontend_pieces(d, d_i, fwd, ch, par):
                    p()

            def middle(d, d_i, fwd, ch):
                for dib in range(NDIB):
                    base = dib * 262
                    for j in range(4):
                        off = j if fwd else 6 - j
                        wcol = cw_sb[:, (d_i * NDIB + dib) * 4 + j:(d_i * NDIB + dib) * 4 + j + 1]
                        win = xi_sb[:, base + off: base + off + TC]
                        if j == 0:
                            E("v", lambda v, win=win, wcol=wcol: v.tensor_scalar(
                                out=cacc[0][:], in0=win, scalar1=wcol, scalar2=None,
                                op0=OP.mult), ["xi", "cw_sb"], ["cacc0"])
                        else:
                            src, dst = cacc[(j + 1) % 2], cacc[j % 2]
                            E("v", lambda v, win=win, wcol=wcol, src=src, dst=dst:
                              v.scalar_tensor_tensor(out=dst[:], in0=win, scalar=wcol,
                                                     in1=src[:], op0=OP.mult, op1=OP.add),
                              ["xi", "cw_sb", f"cacc{(j + 1) % 2}"], [f"cacc{j % 2}"])
                    if fwd:
                        E("v", lambda v, base=base: v.tensor_copy(
                            out=xi_sb[:, base:base + 3],
                            in_=xi_sb[:, base + TC: base + TC + 3]), ["xi"], ["xi"])
                    else:
                        E("v", lambda v, base=base: v.tensor_copy(
                            out=xi_sb[:, base + 259: base + 262],
                            in_=xi_sb[:, base + 3: base + 6]), ["xi"], ["xi"])
                    cbcol = cb_sb[:, d_i * NDIB + dib: d_i * NDIB + dib + 1]
                    E("s", lambda s, dib=dib, cbcol=cbcol: s.activation(
                        out=xcb_sb[:, dib * TC:(dib + 1) * TC], in_=cacc[1][:],
                        func=AF.Silu, bias=cbcol), ["cacc1", "cb_sb"], ["xcb"])
                for kc in range(NDIB):
                    E("t", lambda t, kc=kc: t.matmul(
                        out=pC[0:96, 0:TC],
                        lhsT=xpw_sb[:, (d_i * NDIB + kc) * 96:(d_i * NDIB + kc + 1) * 96],
                        rhs=xcb_sb[:, kc * TC:(kc + 1) * TC],
                        start=(kc == 0), stop=(kc == NDIB - 1)),
                      ["xpw_sb", "xcb"], ["pC"] if kc in (0, NDIB - 1) else [])
                E("s", lambda s: s.copy(out=xdbl[:], in_=pC[0:96, 0:TC]), ["pC"], ["xdbl"])
                E("s", lambda s: s.copy(out=xdblb[:], in_=pC[0:64, 0:TC]), ["pC"], ["xdblb"])
                E("v", lambda v: v.tensor_copy(out=bcsrc[:], in_=pC[64:96, 0:TC]),
                  ["pC"], ["bcsrc"])
                E.dma("gb", bcflat[0:1, :], bcsrc[:], ["bcsrc"], ["bcflat"])
                for bi, (tgt, tk) in enumerate(((Bb_sb, "Bb"), (Cb_sb, "Cb"))):
                    for pr in range(8):
                        for q_ in range(2):
                            n = pr * 2 + q_
                            E("t", lambda t, bi=bi, n=n, q_=q_: t.matmul(
                                out=pC[:, q_ * 256: q_ * 256 + TC],
                                lhsT=ones_bf[:],
                                rhs=bcflat[0:1, (16 * bi + n) * TC:(16 * bi + n + 1) * TC],
                                start=True, stop=True),
                              ["ones", "bcflat"], ["pC"])
                            if TC > 256:
                                break
                        E("s", lambda s, tgt=tgt, pr=pr: s.copy(
                            out=tgt[:, pr * 2 * TC: pr * 2 * TC + min(2 * TC, 512)],
                            in_=pC[:, 0:min(2 * TC, 512)]), ["pC"], [tk])

            def scan_head(d, d_i, fwd, ch, dib):
                dp = dib % 2
                E("t", lambda t, dib=dib: t.matmul(
                    out=pD[:, 0:TC],
                    lhsT=dtw_sb[:, d_i * DI + dib * 128: d_i * DI + (dib + 1) * 128],
                    rhs=xdblb[:, 0:TC], start=True, stop=True),
                  ["dtw_sb", "xdblb"], ["pD"])
                dtbcol = dtb_sb[:, d_i * NDIB + dib: d_i * NDIB + dib + 1]
                E("s", lambda s, dtbcol=dtbcol: s.activation(
                    out=sp_s[:], in_=pD[:, 0:TC], func=AF.Exp, bias=dtbcol),
                  ["pD", "dtb_sb"], ["sp_s"])
                E("s", lambda s, dp=dp: s.activation(
                    out=dts[dp][:], in_=sp_s[:],
                    func=AF.Ln, bias=ko_sb[:, 1:2]), ["sp_s", "ko"], [f"dts{dp}"])

            def scan_dib(d, d_i, fwd, ch, par, dib, skip_head=False):
                off_real = 1 if fwd else 0
                off_seam = 0 if fwd else TC
                st_off = TC if fwd else 0
                dp = dib % 2
                if not skip_head:
                    scan_head(d, d_i, fwd, ch, dib)
                E("v", lambda v, dp=dp, dib=dib: v.tensor_mul(
                    us_s[:], dts[dp][:],
                    xcb_sb[:, dib * TC:(dib + 1) * TC]), [f"dts{dp}", "xcb"], ["us"])
                yp = dib % 2
                ddcol = dd_sb[:, d_i * NDIB + dib: d_i * NDIB + dib + 1]
                E("v", lambda v, dib=dib, ddcol=ddcol, yp=yp: v.tensor_scalar(
                    out=ys_s[yp][:], in0=xcb_sb[:, dib * TC:(dib + 1) * TC],
                    scalar1=ddcol, scalar2=None, op0=OP.mult), ["xcb", "dd_sb"], [f"ys{yp}"])
                for h in range(2):
                    hp = (dib * 2 + h) % 2
                    n0 = NG * h
                    for k in range(NG):
                        n = n0 + k
                        ai = (d_i * NDIB + dib) * NST + n
                        acol = a_sb[:, ai:ai + 1]
                        E("s", lambda s, hp=hp, k=k, dp=dp, acol=acol, off_real=off_real:
                          s.activation(
                            out=dAx[hp][:, k * (TC + 1) + off_real:
                                         k * (TC + 1) + off_real + TC],
                            in_=dts[dp][:], func=AF.Exp, scale=acol),
                          [f"dts{dp}", "a_sb"], [f"dAx{hp}"])
                    u_rep = bass.AP(us_s, 0, [[TC, 128], [0, NG], [1, TC]])
                    bb_ap = bass.AP(Bb_sb, n0 * TC, [[NST * TC, 128], [TC, NG], [1, TC]])
                    ub_out = bass.AP(uBx, off_real, [[EXT, 128], [TC + 1, NG], [1, TC]])
                    E("v", lambda v, u_rep=u_rep, bb_ap=bb_ap, ub_out=ub_out:
                      v.tensor_tensor(out=ub_out, in0=u_rep, in1=bb_ap, op=OP.mult),
                      ["us", "Bb"], ["uBx"])
                    seam_out = bass.AP(uBx, off_seam, [[EXT, 128], [TC + 1, NG]])
                    E("v", lambda v, seam_out=seam_out, dib=dib, n0=n0: v.tensor_copy(
                        out=seam_out, in_=st_sb[:, dib * NST + n0: dib * NST + n0 + NG]),
                      ["st", "uBx"], ["uBx"])
                    if fwd:
                        E("v", lambda v, hp=hp: v.tensor_tensor_scan(
                            out=Hx[hp][:], data0=dAx[hp][:], data1=uBx[:],
                            initial=0.0, op0=OP.mult, op1=OP.add),
                          [f"dAx{hp}", "uBx"], ["Hx0"])
                    else:
                        E("v", lambda v, hp=hp: v.tensor_tensor_scan(
                            out=rev_ap(Hx[hp], 128, EXT),
                            data0=rev_ap(dAx[hp], 128, EXT),
                            data1=rev_ap(uBx, 128, EXT),
                            initial=0.0, op0=OP.mult, op1=OP.add),
                          [f"dAx{hp}", "uBx"], ["Hx0"])
                    st_in = bass.AP(Hx[hp], st_off, [[EXT, 128], [TC + 1, NG]])
                    E("v", lambda v, st_in=st_in, dib=dib, n0=n0: v.tensor_copy(
                        out=st_sb[:, dib * NST + n0: dib * NST + n0 + NG], in_=st_in),
                      ["Hx0"], ["st"])
                    h_ap = bass.AP(Hx[hp], off_real, [[EXT, 128], [TC + 1, NG], [1, TC]])
                    cb_ap = bass.AP(Cb_sb, n0 * TC, [[NST * TC, 128], [TC, NG], [1, TC]])
                    E("v", lambda v, h_ap=h_ap, cb_ap=cb_ap, n0=n0:
                      v.tensor_tensor(out=hcn[:, n0 * TC:(n0 + NG) * TC],
                                      in0=h_ap, in1=cb_ap, op=OP.mult),
                      ["Hx0", "Cb"], ["hcn"])
                w = NST * TC // 2
                while w >= TC:
                    E("v", lambda v, w=w: v.tensor_add(
                        hcn[:, 0:w], hcn[:, 0:w], hcn[:, w:2 * w]), ["hcn"], ["hcn"])
                    w //= 2
                E("v", lambda v, yp=yp: v.tensor_add(ys_s[yp][:], ys_s[yp][:], hcn[:, 0:TC]),
                  ["hcn", f"ys{yp}"], [f"ys{yp}"])
                E("v", lambda v, dib=dib, yp=yp: v.tensor_mul(
                    ysz_sb[:, dib * TC:(dib + 1) * TC], ys_s[yp][:],
                    szb_sb[par][:, dib * TC:(dib + 1) * TC]),
                  [f"ys{yp}", f"szb{par}"], ["ysz"])

            def outproj(d, d_i, fwd, ch, par):
                t0 = ch * TC
                if not fwd:
                    for dmb in range(8):
                        E.dma("qb", af_sb[:, dmb * TC:(dmb + 1) * TC],
                              attn_d[dmb * 128:(dmb + 1) * 128, t0:t0 + TC],
                              [f"attnd{ch}"], ["af"])
                def wf_load(dmb):
                    op_ = dmb % 3
                    E.dma("gb", wf_b[op_][:], wft_d[d][:, dmb * NDIB * 128:(dmb + 1) * NDIB * 128],
                          [], [f"wf{op_}"])
                for dmb in range(3):
                    wf_load(dmb)
                for dmb in range(8):
                    op_ = dmb % 3
                    wb, wbk = wf_b[op_], f"wf{op_}"
                    ps, pk = (pE, "pE") if dmb % 2 == 0 else (pF, "pF")
                    for dib in range(NDIB):
                        E("t", lambda t, dib=dib, ps=ps, wb=wb: t.matmul(
                            out=ps[:, 0:TC],
                            lhsT=wb[:, dib * 128:(dib + 1) * 128],
                            rhs=ysz_sb[:, dib * TC:(dib + 1) * TC],
                            start=(dib == 0), stop=(dib == NDIB - 1)),
                          [wbk, "ysz"], [pk] if dib in (0, NDIB - 1) else [])
                    if dmb + 3 < 8:
                        wf_load(dmb + 3)
                    if fwd:
                        E("s", lambda s, ps=ps: s.copy(out=aev[:], in_=ps[:, 0:TC]),
                          [pk], ["aev"])
                        E.dma("qc", attn_d[dmb * 128:(dmb + 1) * 128, t0:t0 + TC],
                              aev[:], ["aev"], [f"attnd{ch}"])
                    else:
                        pbcol = pb_sb[:, dmb:dmb + 1]
                        E("v", lambda v, ps=ps, dmb=dmb, pbcol=pbcol: v.scalar_tensor_tensor(
                            out=aev[:], in0=ps[:, 0:TC], scalar=pbcol,
                            in1=af_sb[:, dmb * TC:(dmb + 1) * TC],
                            op0=OP.add, op1=OP.add), [pk, "pb_sb", "af"], ["aev"])
                        for tt in range(2):
                            E("t", lambda t, tt=tt: t.transpose(
                                out=pT2[:, tt * 128:(tt + 1) * 128],
                                in_=aev[:, tt * 128:(tt + 1) * 128],
                                identity=id_sb[:]), ["aev", "id_sb"], ["pT2"])
                        for tt in range(2):
                            E("v", lambda v, tt=tt, dmb=dmb: v.tensor_add(
                                x_t[par][tt][:, dmb * 128:(dmb + 1) * 128],
                                x_t[par][tt][:, dmb * 128:(dmb + 1) * 128],
                                pT2[:, tt * 128:(tt + 1) * 128]),
                              ["pT2", f"x_t{par}{tt}"], [f"x_t{par}{tt}"])
                if not fwd:
                    for tt in range(2):
                        rows = slice(t0 + tt * 128, t0 + tt * 128 + 128)
                        E.dma("qc", x2_d[rows, :], x_t[par][tt][:], [f"x_t{par}{tt}"], [])

            for d_i, d in enumerate(_dirs):
                fwd = d == "f"
                E("v", lambda v: v.memset(st_sb[:], 0.0), ["st"], ["st"])
                E("v", lambda v: v.memset(dAx[0][:], 0.0), ["dAx0"], ["dAx0"])
                E("v", lambda v: v.memset(dAx[1][:], 0.0), ["dAx1"], ["dAx1"])
                E("v", lambda v: v.memset(xi_sb[:], 0.0), ["xi"], ["xi"])
                chunks = list(range(_nch)) if fwd else list(range(_nch - 1, -1, -1))
                frontend(d, d_i, fwd, chunks[0], 0)
                middle(d, d_i, fwd, chunks[0])
                scan_head(d, d_i, fwd, chunks[0], 0)
                for ci, ch in enumerate(chunks):
                    par = ci % 2
                    scan_dib(d, d_i, fwd, ch, par, 0, skip_head=True)
                    fe = (frontend_pieces(d, d_i, fwd, chunks[ci + 1], 1 - par)
                          if ci + 1 < len(chunks) else [])
                    fi = 0
                    for dib in range(1, NDIB):
                        scan_dib(d, d_i, fwd, ch, par, dib)
                        take = max(0, min(len(fe) - fi,
                                          (len(fe) * dib) // (NDIB - 1) - fi))
                        for _ in range(take):
                            fe[fi]()
                            fi += 1
                    while fi < len(fe):
                        fe[fi]()
                        fi += 1
                    if ci + 1 < len(chunks):
                        middle(d, d_i, fwd, chunks[ci + 1])
                        scan_head(d, d_i, fwd, chunks[ci + 1], 0)
                    outproj(d, d_i, fwd, ch, par)
            E.barrier_all()

        if "F" not in _phases:
            with nc.sbuf_tensor("zo", [128, DM], F32) as zo:
                E("v", lambda v: v.memset(zo[:], 0.0), [], ["zo"])
                for tile in range(T // 128):
                    E.dma("qc", out_h[tile * 128:(tile + 1) * 128, :], zo[:], ["zo"], [])
            for i in range(E.DMA_STREAMS["qc"][1]):
                sub = f"qc{i}"
                if E.cnt[sub]:
                    nc.sync.wait_ge(E.sem[sub], E.cnt[sub])
            return nc
        with nc.sbuf_tensor("ternb", [128, 8 * DM], BF16) as ternb, \
             nc.sbuf_tensor("tld", [128, 8 * DM], I32) as tld, \
             nc.sbuf_tensor("tcv", [128, 4096], F32) as tcv, \
             nc.sbuf_tensor("fb_bc", [128, DM], F32) as fb_bc, \
             nc.sbuf_tensor("sc_c", [128, 2], F32) as sc_c, \
             nc.sbuf_tensor("x2_t", [128, DM], F32) as x2_t, \
             nc.sbuf_tensor("f_ln", [128, 2 * DM], F32) as f_ln, \
             nc.sbuf_tensor("f_c", [128, 8], F32) as f_c, \
             nc.sbuf_tensor("xq_tm", [128, DM], BF16) as xq_tm, \
             nc.sbuf_tensor("xq_fm", [128, DM], BF16) as xq_fm, \
             nc.sbuf_tensor("f_ep", [128, 512], F32) as f_ep, \
             nc.sbuf_tensor("ofin", [128, DM], F32) as ofin:
            E.dma("qa", tld[:], tern_h, [], ["tld"])
            for j in range(2):
                sl = slice(j * 4096, (j + 1) * 4096)
                E("v", lambda v, sl=sl: v.tensor_copy(out=tcv[:], in_=tld[:, sl]),
                  ["tld"], ["tcv"])
                E("v", lambda v, sl=sl: v.tensor_scalar_add(
                    ternb[:, sl], tcv[:], -1.0), ["tcv"], ["ternb"])
            E.dma("qa", fb_bc[0:1, :], fbias_h, [], ["fb"])
            n_bc = 1
            while n_bc < 128:
                m = min(n_bc, 128 - n_bc)
                E.dma("qa", fb_bc[n_bc:n_bc + m, :], fb_bc[0:m, :], ["fb"], ["fb"])
                n_bc += m
            E.dma("qa", sc_c[0:1, 0:1], ascale_h, [], ["sc"])
            n_bc = 1
            while n_bc < 128:
                m = min(n_bc, 128 - n_bc)
                E.dma("qa", sc_c[n_bc:n_bc + m, 0:1], sc_c[0:m, 0:1], ["sc"], ["sc"])
                n_bc += m
            E("v", lambda v: v.tensor_scalar_max(sc_c[:, 0:1], sc_c[:, 0:1], 1e-5),
              ["sc"], ["sc"])
            E("v", lambda v: v.reciprocal(out=sc_c[:, 1:2], in_=sc_c[:, 0:1]), ["sc"], ["sc"])

            for tile in range(T // 128):
                rows = slice(tile * 128, tile * 128 + 128)
                E.dma("qb", x2_t[:], x2_d[rows, :], [], ["x2t"])
                E("v", lambda v: v.reduce_sum(f_c[:, 0:1], x2_t[:], axis=mybir.AxisListType.X),
                  ["x2t"], ["f_c"])
                E("v", lambda v: v.tensor_scalar_mul(f_c[:, 1:2], f_c[:, 0:1], -1.0 / DM),
                  ["f_c"], ["f_c"])
                E("v", lambda v: v.tensor_scalar_add(f_ln[:, 0:DM], x2_t[:], f_c[:, 1:2]),
                  ["x2t", "f_c"], ["f_ln"])
                E("s", lambda s: s.activation(out=f_ln[:, DM:2 * DM], in_=f_ln[:, 0:DM],
                                              func=AF.Square, accum_out=f_c[:, 2:3]),
                  ["f_ln"], ["f_ln", "f_c"])
                E("s", lambda s: s.activation(out=f_c[:, 3:4], in_=f_c[:, 2:3], func=AF.Sqrt,
                                              scale=ko_sb[:, 2:3], bias=ko_sb[:, 0:1]),
                  ["f_c", "ko"], ["f_c"])
                E("v", lambda v: v.reciprocal(out=f_c[:, 4:5], in_=f_c[:, 3:4]), ["f_c"], ["f_c"])
                E("v", lambda v: v.tensor_scalar_mul(f_ln[:, 0:DM], f_ln[:, 0:DM], f_c[:, 4:5]),
                  ["f_ln", "f_c"], ["f_ln"])
                E("v", lambda v: v.tensor_scalar(
                    out=f_ln[:, DM:2 * DM], in0=f_ln[:, 0:DM], scalar1=sc_c[:, 1:2],
                    scalar2=MAGIC, op0=OP.mult, op1=OP.add), ["f_ln", "sc"], ["f_ln"])
                E("v", lambda v: v.tensor_scalar(
                    out=f_ln[:, 0:DM], in0=f_ln[:, DM:2 * DM], scalar1=MAGIC,
                    scalar2=127.0, op0=OP.subtract, op1=OP.min), ["f_ln"], ["f_ln"])
                E("v", lambda v: v.tensor_scalar_max(xq_tm[:], f_ln[:, 0:DM], -127.0),
                  ["f_ln"], ["xq_tm"])
                for dmb in range(8):
                    E("t", lambda t, dmb=dmb: t.transpose(
                        out=pT2[:, dmb * 128:(dmb + 1) * 128],
                        in_=xq_tm[:, dmb * 128:(dmb + 1) * 128], identity=id_sb[:]),
                      ["xq_tm", "id_sb"], ["pT2"])
                for dmb in range(8):
                    E("s", lambda s, dmb=dmb: s.copy(
                        out=xq_fm[:, dmb * 128:(dmb + 1) * 128],
                        in_=pT2[:, dmb * 128:(dmb + 1) * 128]), ["pT2"], ["xq_fm"])
                for half in range(2):
                    ps, pk = (pA, "pA") if half == 0 else (pB, "pB")
                    for kc in range(8):
                        E("t", lambda t, kc=kc, half=half, ps=ps: t.matmul(
                            out=ps[:, 0:512],
                            lhsT=xq_fm[:, kc * 128:(kc + 1) * 128],
                            rhs=ternb[:, kc * DM + half * 512: kc * DM + (half + 1) * 512],
                            start=(kc == 0), stop=(kc == 7)),
                          ["xq_fm", "ternb"], [pk] if kc in (0, 7) else [])
                    E("v", lambda v, ps=ps, half=half: v.scalar_tensor_tensor(
                        out=f_ep[:], in0=ps[:, 0:512], scalar=sc_c[:, 0:1],
                        in1=fb_bc[:, half * 512:(half + 1) * 512],
                        op0=OP.mult, op1=OP.add), [pk, "sc", "fb"], ["f_ep"])
                    E("v", lambda v, half=half: v.tensor_add(
                        ofin[:, half * 512:(half + 1) * 512], f_ep[:],
                        x2_t[:, half * 512:(half + 1) * 512]), ["f_ep", "x2t"], ["ofin"])
                E.dma("qc", out_h[rows, :], ofin[:], ["ofin"], [])
            for i in range(E.DMA_STREAMS["qc"][1]):
                sub = f"qc{i}"
                if E.cnt[sub]:
                    nc.sync.wait_ge(E.sem[sub], E.cnt[sub])

    return nc


_NC_CACHE = None


def kernel(**inputs):
    global _NC_CACHE
    B = 8
    xs = np.ascontiguousarray(np.asarray(inputs["L0_attn_x"]))

    def lay_pblocks(w):
        r, c = w.shape
        k = r // 128
        return np.ascontiguousarray(w.reshape(k, 128, c).transpose(1, 0, 2).reshape(128, k * c))

    ins = {}
    for d, p in (("f", "fwd"), ("b", "bwd")):
        inwT = np.asarray(inputs[f"L0_attn_{p}_in_W"]).T
        t = inwT.reshape(8, 128, 32, 128).transpose(1, 2, 0, 3).reshape(128, 8 * 32 * 128)
        ins[f"inw_{d}"] = np.ascontiguousarray(t)
        ins[f"outw_{d}"] = lay_pblocks(np.asarray(inputs[f"L0_attn_{p}_out_W"]))
    ins["projt"] = lay_pblocks(np.ascontiguousarray(np.asarray(inputs["L0_attn_proj_W"]).T))
    xp = np.stack([np.asarray(inputs["L0_attn_fwd_xproj_W"]).T,
                   np.asarray(inputs["L0_attn_bwd_xproj_W"]).T])
    xp = xp.reshape(2, 16, 128, 96).transpose(2, 0, 1, 3).reshape(128, 2 * 16 * 96)
    ins["xpw"] = np.ascontiguousarray(xp)
    dtw = np.stack([np.asarray(inputs["L0_attn_fwd_dt_W"]).T,
                    np.asarray(inputs["L0_attn_bwd_dt_W"]).T])
    ins["dtw"] = np.ascontiguousarray(dtw.transpose(1, 0, 2).reshape(64, 2 * 2048))

    def two_dir(name, inner):
        s = np.stack([np.asarray(inputs[f"L0_attn_fwd_{name}"]).reshape(DI, inner),
                      np.asarray(inputs[f"L0_attn_bwd_{name}"]).reshape(DI, inner)])
        s = s.reshape(2, 16, 128, inner).transpose(2, 0, 1, 3).reshape(128, 2 * 16 * inner)
        return np.ascontiguousarray(s)

    ins["cw"] = two_dir("conv_w", 4)
    ins["cb"] = two_dir("conv_b", 1)
    ins["dtb"] = two_dir("dt_b", 1)
    ins["alog"] = two_dir("A_log", NST)
    ins["dd"] = two_dir("D", 1)
    ins["pb"] = np.ascontiguousarray(np.asarray(inputs["L0_attn_proj_b"]).reshape(8, 128).T)
    ins["tern"] = lay_pblocks(np.ascontiguousarray(
        np.asarray(inputs["L0_ffn_ternary_weight"]).T))
    ins["fbias"] = np.ascontiguousarray(np.asarray(inputs["L0_ffn_bias"]).reshape(1, DM))
    ins["ascale"] = np.ascontiguousarray(
        np.asarray(inputs["L0_ffn_act_scale"], dtype=np.float32).reshape(1, 1))
    ins["ident"] = np.eye(128, dtype=ml_dtypes.bfloat16)

    if _NC_CACHE is None:
        _NC_CACHE = build_nc()
    nc = _NC_CACHE

    in_maps = []
    for b in range(B):
        m = dict(ins)
        m["x"] = np.ascontiguousarray(xs[b])
        in_maps.append(m)
    res = run_bass_kernel_spmd(nc, in_maps, core_ids=list(range(B)))
    return np.stack([np.asarray(r["out"], dtype=np.float32) for r in res.results], axis=0)


# version 16
# speedup vs baseline: 1.2574x; 1.2574x over previous
import os
import numpy as np
import ml_dtypes
import concourse.bass as bass
import concourse.mybir as mybir
from concourse.bass_utils import run_bass_kernel_spmd
from contextlib import ExitStack

F32 = mybir.dt.float32
BF16 = mybir.dt.bfloat16
I32 = mybir.dt.int32
AF = mybir.ActivationFunctionType
OP = mybir.AluOpType

T = 4096
DM = 1024
DI = 2048
NST = 16
TC = 256
NCH = T // TC
NDIB = DI // 128
MAGIC = 12582912.0


class Emit:

    DMA_STREAMS = {"qa": ("q", 8), "qb": ("q", 6), "qc": ("q", 6), "gb": ("g", 6)}

    def __init__(self, nc, es):
        self.nc = nc
        self.eng = {"v": nc.vector, "s": nc.scalar, "t": nc.tensor,
                    "q": nc.sync, "g": nc.gpsimd}
        self.sem = {k: es.enter_context(nc.semaphore("sem_" + k)) for k in "vstg"}
        self.cnt = {k: 0 for k in "vstg"}
        for st, (_, r) in self.DMA_STREAMS.items():
            for i in range(r):
                k = f"{st}{i}"
                self.sem[k] = es.enter_context(nc.semaphore("sem_" + k))
                self.cnt[k] = 0
        self.dtotal = {st: 0 for st in self.DMA_STREAMS}
        self.track = {}
        self.excl = {"pA", "pB", "pC", "pD", "pE", "pF", "pT1", "pT2"}
        self.satisfied = {k: {} for k in self.eng}

    def _need(self, e, reads, writes):
        want = {}

        def add(tok):
            if tok is None:
                return
            s, v = tok
            if v > want.get(s, -1):
                want[s] = v
        for k in reads:
            w, rs = self.track.get(k, (None, []))
            add(w)
            if k in self.excl:
                for r in rs:
                    add(r)
        for k in writes:
            w, rs = self.track.get(k, (None, []))
            add(w)
            for r in rs:
                add(r)
        sat = self.satisfied[e]
        return {s: v for s, v in want.items() if v > sat.get(s, -1)}

    def _mark(self, e, need):
        sat = self.satisfied[e]
        for s, v in need.items():
            sat[s] = max(sat.get(s, -1), v)

    def __call__(self, e, make, reads=(), writes=()):
        need = self._need(e, reads, writes)
        items = sorted(need.items())
        for s, v in items[1:]:
            self.eng[e].wait_ge(self.sem[s], v)
        inst = make(self.eng[e])
        if items:
            inst._wait_ge(self.sem[items[0][0]], items[0][1])
        self._mark(e, need)
        inst.then_inc(self.sem[e], 1)
        self.cnt[e] += 1
        tok = (e, self.cnt[e])
        for k in reads:
            w, rs = self.track.get(k, (None, []))
            rs.append(tok)
            self.track[k] = (w, rs)
        for k in writes:
            self.track[k] = (tok, [])
        return tok

    def dma(self, stream, out, in_, reads=(), writes=()):
        qe, r = self.DMA_STREAMS[stream]
        sub = f"{stream}{self.dtotal[stream] % r}"
        self.dtotal[stream] += 1
        need = self._need(qe, reads, writes)
        if self.cnt[sub] > 0 and need.get(sub, -1) < self.cnt[sub]:
            need[sub] = self.cnt[sub]
        for s, v in sorted(need.items()):
            self.eng[qe].wait_ge(self.sem[s], v)
        inst = self.eng[qe].dma_start(out=out, in_=in_)
        self._mark(qe, need)
        inst.then_inc(self.sem[sub], 16)
        self.cnt[sub] += 16
        tok = (sub, self.cnt[sub])
        for k in reads:
            w, rs = self.track.get(k, (None, []))
            rs.append(tok)
            self.track[k] = (w, rs)
        for k in writes:
            self.track[k] = (tok, [])
        return tok

    def barrier_all(self):
        for e in self.eng:
            sat = self.satisfied[e]
            for s, c in self.cnt.items():
                if c > sat.get(s, -1):
                    self.eng[e].wait_ge(self.sem[s], c)
                    sat[s] = c


def rev_ap(handle, parts, width):
    return bass.AP(handle, width - 1, [[width, parts], [-1, width]])


def build_nc():
    _phases = os.environ.get("KMK_PHASES", "0MF")
    _nch = int(os.environ.get("KMK_CHUNKS", str(NCH)))
    _dirs = os.environ.get("KMK_DIRS", "fb")
    _stg = os.environ.get("KMK_STAGES", "LICXEY12DSO")
    nc = bass.Bass("TRN2", target_bir_lowering=False)
    x_h = nc.dram_tensor("x", [T, DM], F32, kind="ExternalInput").ap()
    inw_h = {d: nc.dram_tensor(f"inw_{d}", [128, 8 * 32 * 128], F32, kind="ExternalInput").ap()
             for d in "fb"}
    xpw_h = nc.dram_tensor("xpw", [128, 2 * NDIB * 96], F32, kind="ExternalInput").ap()
    dtw_h = nc.dram_tensor("dtw", [64, 2 * DI], F32, kind="ExternalInput").ap()
    outw_h = {d: nc.dram_tensor(f"outw_{d}", [128, 8 * DI], F32, kind="ExternalInput").ap()
              for d in "fb"}
    projt_h = nc.dram_tensor("projt", [128, 2 * 8 * DM], F32, kind="ExternalInput").ap()
    cw_h = nc.dram_tensor("cw", [128, 2 * NDIB * 4], F32, kind="ExternalInput").ap()
    cb_h = nc.dram_tensor("cb", [128, 2 * NDIB], F32, kind="ExternalInput").ap()
    dtb_h = nc.dram_tensor("dtb", [128, 2 * NDIB], F32, kind="ExternalInput").ap()
    alog_h = nc.dram_tensor("alog", [128, 2 * NDIB * NST], F32, kind="ExternalInput").ap()
    dd_h = nc.dram_tensor("dd", [128, 2 * NDIB], F32, kind="ExternalInput").ap()
    pb_h = nc.dram_tensor("pb", [128, 8], F32, kind="ExternalInput").ap()
    tern_h = nc.dram_tensor("tern", [128, 8 * DM], I32, kind="ExternalInput").ap()
    fbias_h = nc.dram_tensor("fbias", [1, DM], F32, kind="ExternalInput").ap()
    ascale_h = nc.dram_tensor("ascale", [1, 1], F32, kind="ExternalInput").ap()
    ident_h = nc.dram_tensor("ident", [128, 128], BF16, kind="ExternalInput").ap()

    out_h = nc.dram_tensor("out", [T, DM], F32, kind="ExternalOutput").ap()

    inwbf_d = {d: nc.dram_tensor(f"inwbf_{d}", [128, 8 * 32 * 128], BF16, kind="Internal").ap()
               for d in "fb"}
    wft_d = {d: nc.dram_tensor(f"wft_{d}", [128, 8 * NDIB * 128], BF16, kind="Internal").ap()
             for d in "fb"}
    attn_d = nc.dram_tensor("attn", [DM, T], BF16, kind="Internal").ap()
    x2_d = nc.dram_tensor("x2", [T, DM], F32, kind="Internal").ap()

    with ExitStack() as es:
        E = Emit(nc, es)
        pA = es.enter_context(nc.psum_tensor("pA", [128, 512], F32))
        pB = es.enter_context(nc.psum_tensor("pB", [128, 512], F32))
        pC = es.enter_context(nc.psum_tensor("pC", [128, 512], F32))
        pD = es.enter_context(nc.psum_tensor("pD", [128, 512], F32))
        pE = es.enter_context(nc.psum_tensor("pE", [128, 512], F32))
        pF = es.enter_context(nc.psum_tensor("pF", [128, 512], F32))
        pT1 = es.enter_context(nc.psum_tensor("pT1", [128, 1024], BF16))
        pT2 = es.enter_context(nc.psum_tensor("pT2", [128, 1024], BF16))

        id_sb = es.enter_context(nc.sbuf_tensor("id_sb", [128, 128], BF16))
        cw_sb = es.enter_context(nc.sbuf_tensor("cw_sb", [128, 2 * NDIB * 4], F32))
        cb_sb = es.enter_context(nc.sbuf_tensor("cb_sb", [128, 2 * NDIB], F32))
        dtb_sb = es.enter_context(nc.sbuf_tensor("dtb_sb", [128, 2 * NDIB], F32))
        a_sb = es.enter_context(nc.sbuf_tensor("a_sb", [128, 2 * NDIB * NST], F32))
        dd_sb = es.enter_context(nc.sbuf_tensor("dd_sb", [128, 2 * NDIB], F32))
        pb_sb = es.enter_context(nc.sbuf_tensor("pb_sb", [128, 8], F32))
        xpw_sb = es.enter_context(nc.sbuf_tensor("xpw_sb", [128, 2 * NDIB * 96], BF16))
        dtw_sb = es.enter_context(nc.sbuf_tensor("dtw_sb", [64, 2 * DI], BF16))
        st_sb = es.enter_context(nc.sbuf_tensor("st_sb", [128, NDIB * NST], F32))
        ko_sb = es.enter_context(nc.sbuf_tensor("ko_sb", [128, 4], F32))
        ones_bf = es.enter_context(nc.sbuf_tensor("ones_bf", [1, 128], BF16))
        inw_b = [es.enter_context(nc.sbuf_tensor(f"inw_b{i}", [128, 1024], BF16))
                 for i in range(2)]
        wf_b = [es.enter_context(nc.sbuf_tensor(f"wf_b{i}", [128, 2048], BF16))
                for i in range(3)]

        with nc.sbuf_tensor("w_ld", [128, 4096], F32) as w_ld, \
             nc.sbuf_tensor("w_cst", [128, 4096], BF16) as w_cst, \
             nc.sbuf_tensor("projt_sb", [128, 8 * DM], BF16) as projt_sb, \
             nc.sbuf_tensor("outw_sb", [128, 8 * DI], BF16) as outw_sb, \
             nc.sbuf_tensor("wfev", [128, 1024], BF16) as wfev:
            E.dma("qa", id_sb[:], ident_h, [], ["id_sb"])
            E("v", lambda v: v.memset(ko_sb[:, 0:1], 1e-5), [], ["ko"])
            E("v", lambda v: v.memset(ko_sb[:, 1:2], 1.0), ["ko"], ["ko"])
            E("v", lambda v: v.memset(ko_sb[:, 2:3], 1.0 / DM), ["ko"], ["ko"])
            E("v", lambda v: v.memset(ones_bf[:], 1.0), [], ["ones"])
            E.dma("qa", cw_sb[:], cw_h, [], ["cw_sb"])
            E.dma("qa", cb_sb[:], cb_h, [], ["cb_sb"])
            E.dma("qa", dtb_sb[:], dtb_h, [], ["dtb_sb"])
            E.dma("qa", dd_sb[:], dd_h, [], ["dd_sb"])
            E.dma("qa", pb_sb[:], pb_h, [], ["pb_sb"])
            E.dma("qa", a_sb[:], alog_h, [], ["a_sb"])
            E("s", lambda s: s.activation(out=a_sb[:], in_=a_sb[:], func=AF.Exp),
              ["a_sb"], ["a_sb"])
            E("v", lambda v: v.tensor_scalar_mul(a_sb[:], a_sb[:], -1.0),
              ["a_sb"], ["a_sb"])
            E.dma("qa", w_ld[:, 0:2 * NDIB * 96], xpw_h, [], ["w_ld"])
            E("v", lambda v: v.tensor_copy(out=xpw_sb[:], in_=w_ld[:, 0:2 * NDIB * 96]),
              ["w_ld"], ["xpw_sb"])
            E.dma("qa", w_ld[0:64, 0:2 * DI], dtw_h, [], ["w_ld"])
            E("v", lambda v: v.tensor_copy(out=dtw_sb[:], in_=w_ld[0:64, 0:2 * DI]),
              ["w_ld"], ["dtw_sb"])
            for d in "fb":
                for j in range(8):
                    sl = slice(j * 4096, (j + 1) * 4096)
                    E.dma("qa", w_ld[:], inw_h[d][:, sl], [], ["w_ld"])
                    E("v", lambda v: v.tensor_copy(out=w_cst[:], in_=w_ld[:]),
                      ["w_ld"], ["w_cst"])
                    E.dma("qa", inwbf_d[d][:, sl], w_cst[:], ["w_cst"], [])
            for di_, d in enumerate("fb"):
                for j in range(4):
                    sl = slice(j * 4096, (j + 1) * 4096)
                    E.dma("qa", w_ld[:], outw_h[d][:, sl], [], ["w_ld"])
                    E("v", lambda v, sl=sl: v.tensor_copy(out=outw_sb[:, sl], in_=w_ld[:]),
                      ["w_ld"], ["outw_sb"])
                for j in range(2):
                    src = projt_h[:, di_ * 8 * DM + j * 4096: di_ * 8 * DM + (j + 1) * 4096]
                    sl = slice(j * 4096, (j + 1) * 4096)
                    E.dma("qa", w_ld[:], src, [], ["w_ld"])
                    E("v", lambda v, sl=sl: v.tensor_copy(out=projt_sb[:, sl], in_=w_ld[:]),
                      ["w_ld"], ["projt_sb"])
                for dib in range(NDIB):
                    for half in range(2):
                        ps, pk = (pA, "pA") if half == 0 else (pB, "pB")
                        for kc in range(8):
                            E("t", lambda t, dib=dib, half=half, kc=kc, ps=ps:
                              t.matmul(out=ps[:, 0:512],
                                       lhsT=outw_sb[:, kc * DI + dib * 128: kc * DI + (dib + 1) * 128],
                                       rhs=projt_sb[:, kc * DM + half * 512: kc * DM + (half + 1) * 512],
                                       start=(kc == 0), stop=(kc == 7)),
                              ["outw_sb", "projt_sb"],
                              [pk] if kc in (0, 7) else [])
                        E("s", lambda s, half=half, ps=ps: s.copy(
                            out=wfev[:, half * 512:(half + 1) * 512], in_=ps[:, 0:512]),
                          [pk], ["wfev"])
                    for dmb in range(8):
                        E.dma("qa", wft_d[d][:, (dmb * NDIB + dib) * 128:(dmb * NDIB + dib + 1) * 128],
                              wfev[:, dmb * 128:(dmb + 1) * 128], ["wfev"], [])
            E.barrier_all()

        with ExitStack() as ms:
          if "M" in _phases:
            def sb(name, shape, dt_=F32):
                return ms.enter_context(nc.sbuf_tensor(name, shape, dt_))
            x_t = [[sb("x_t00", [128, DM]), sb("x_t01", [128, DM])],
                   [sb("x_t10", [128, DM]), sb("x_t11", [128, DM])]]
            ln_s = sb("ln_s", [128, DM])
            ln_c = sb("ln_c", [128, 8])
            xln_tm = sb("xln_tm", [128, DM], BF16)
            xln_fm = sb("xln_fm", [128, 8 * TC], BF16)
            xi_sb = sb("xi_sb", [128, NDIB * 262], BF16)
            szb_sb = [sb("szb0", [128, NDIB * TC], BF16), sb("szb1", [128, NDIB * TC], BF16)]
            xcb_sb = sb("xcb_sb", [128, NDIB * TC], BF16)
            ysz_sb = sb("ysz_sb", [128, NDIB * TC], BF16)
            xdbl = sb("xdbl", [96, TC])
            xdblb = sb("xdblb", [64, TC], BF16)
            bcsrc = sb("bcsrc", [32, TC], BF16)
            bcflat = sb("bcflat", [1, 32 * TC], BF16)
            Bb_sb = sb("Bb_sb", [128, NST * TC], BF16)
            Cb_sb = sb("Cb_sb", [128, NST * TC], BF16)
            NG = 8
            SEG = TC + 2
            EXT = NG * SEG
            dAx = [sb("dAx0", [128, EXT]), sb("dAx1", [128, EXT])]
            uBx = sb("uBx", [128, EXT], BF16)
            _hx = sb("Hx0", [128, EXT])
            Hx = [_hx, _hx]
            hcn = sb("hcn", [128, NST * TC])
            dts = [sb("dts0", [128, TC]), sb("dts1", [128, TC])]
            us_s = sb("us_s", [128, TC], BF16)
            ys_s = [sb("ys_s0", [128, TC]), sb("ys_s1", [128, TC])]
            cacc = [sb("cacc0", [128, TC], BF16), sb("cacc1", [128, TC], BF16)]
            sp_s = sb("sp_s", [128, TC])
            af_sb = sb("af_sb", [128, 8 * TC], BF16)
            aev = sb("aev", [128, TC], BF16)

            def frontend_pieces(d, d_i, fwd, ch, par):
                t0 = ch * TC
                pieces = []
                def ln_tile(tt):
                    rows = slice(t0 + tt * 128, t0 + tt * 128 + 128)
                    xk = f"x_t{par}{tt}"
                    E.dma("qb", x_t[par][tt][:], x_h[rows, :], [], [xk])
                    E("s", lambda s, tt=tt: s.activation(
                        out=ln_s[:], in_=x_t[par][tt][:], func=AF.Square,
                        accum_out=ln_c[:, 2:3]), [xk, "ln_s"], ["ln_s", "ln_c"])
                    E("v", lambda v, tt=tt: v.reduce_sum(
                        ln_c[:, 0:1], x_t[par][tt][:], axis=mybir.AxisListType.X),
                      [xk], ["ln_c"])
                    E("v", lambda v: v.tensor_scalar_mul(
                        ln_c[:, 1:2], ln_c[:, 0:1], -1.0 / DM), ["ln_c"], ["ln_c"])
                    E("v", lambda v: v.tensor_mul(
                        ln_c[:, 5:6], ln_c[:, 1:2], ln_c[:, 1:2]), ["ln_c"], ["ln_c"])
                    E("v", lambda v: v.tensor_scalar(
                        out=ln_c[:, 6:7], in0=ln_c[:, 5:6], scalar1=-1.0, scalar2=1e-5,
                        op0=OP.mult, op1=OP.add), ["ln_c"], ["ln_c"])
                    E("v", lambda v, tt=tt: v.tensor_scalar_add(
                        ln_s[:], x_t[par][tt][:], ln_c[:, 1:2]), [xk, "ln_c", "ln_s"], ["ln_s"])
                    E("s", lambda s: s.activation(
                        out=ln_c[:, 3:4], in_=ln_c[:, 2:3], func=AF.Sqrt,
                        scale=ko_sb[:, 2:3], bias=ln_c[:, 6:7]), ["ln_c", "ko"], ["ln_c"])
                    E("v", lambda v: v.reciprocal(out=ln_c[:, 4:5], in_=ln_c[:, 3:4]),
                      ["ln_c"], ["ln_c"])
                    E("v", lambda v: v.tensor_scalar_mul(
                        xln_tm[:], ln_s[:], ln_c[:, 4:5]), ["ln_s", "ln_c"], ["xln_tm"])
                    for dmb in range(8):
                        E("t", lambda t, dmb=dmb: t.transpose(
                            out=pT1[:, dmb * 128:(dmb + 1) * 128],
                            in_=xln_tm[:, dmb * 128:(dmb + 1) * 128],
                            identity=id_sb[:]), ["xln_tm", "id_sb"], ["pT1"])
                    for dmb in range(8):
                        E("s", lambda s, dmb=dmb, tt=tt: s.copy(
                            out=xln_fm[:, dmb * TC + tt * 128: dmb * TC + tt * 128 + 128],
                            in_=pT1[:, dmb * 128:(dmb + 1) * 128]), ["pT1"], ["xln_fm"])
                for tt in range(2):
                    pieces.append((lambda tt=tt: ln_tile(tt)))
                def ip_dib(dib):
                    ip = dib % 2
                    ib, ibk = inw_b[ip], f"inw{ip}"
                    E.dma("qa", ib[:], inwbf_d[d][:, dib * 1024:(dib + 1) * 1024], [], [ibk])
                    ps, pk = (pA, "pA") if ip == 0 else (pB, "pB")
                    for kc in range(8):
                        E("t", lambda t, kc=kc, ps=ps, ib=ib:
                          t.matmul(out=ps[:, 0:TC],
                                   lhsT=ib[:, kc * 128:(kc + 1) * 128],
                                   rhs=xln_fm[:, kc * TC:(kc + 1) * TC],
                                   start=(kc == 0), stop=(kc == 7)),
                          [ibk, "xln_fm"], [pk] if kc in (0, 7) else [])
                    if dib < NDIB:
                        E("s", lambda s, dib=dib, ps=ps: s.copy(
                            out=xi_sb[:, dib * 262 + 3: dib * 262 + 3 + TC],
                            in_=ps[:, 0:TC]), [pk], ["xi"])
                    else:
                        zb = dib - NDIB
                        E("s", lambda s, zb=zb, ps=ps: s.activation(
                            out=szb_sb[par][:, zb * TC:(zb + 1) * TC], in_=ps[:, 0:TC],
                            func=AF.Silu), [pk], [f"szb{par}"])
                for dib in range(32):
                    pieces.append((lambda dib=dib: ip_dib(dib)))
                return pieces

            def frontend(d, d_i, fwd, ch, par):
                for p in frontend_pieces(d, d_i, fwd, ch, par):
                    p()

            def middle(d, d_i, fwd, ch):
                for dib in range(NDIB):
                    base = dib * 262
                    for j in range(4):
                        off = j if fwd else 6 - j
                        wcol = cw_sb[:, (d_i * NDIB + dib) * 4 + j:(d_i * NDIB + dib) * 4 + j + 1]
                        win = xi_sb[:, base + off: base + off + TC]
                        if j == 0:
                            E("v", lambda v, win=win, wcol=wcol: v.tensor_scalar(
                                out=cacc[0][:], in0=win, scalar1=wcol, scalar2=None,
                                op0=OP.mult), ["xi", "cw_sb"], ["cacc0"])
                        else:
                            src, dst = cacc[(j + 1) % 2], cacc[j % 2]
                            E("v", lambda v, win=win, wcol=wcol, src=src, dst=dst:
                              v.scalar_tensor_tensor(out=dst[:], in0=win, scalar=wcol,
                                                     in1=src[:], op0=OP.mult, op1=OP.add),
                              ["xi", "cw_sb", f"cacc{(j + 1) % 2}"], [f"cacc{j % 2}"])
                    if fwd:
                        E("v", lambda v, base=base: v.tensor_copy(
                            out=xi_sb[:, base:base + 3],
                            in_=xi_sb[:, base + TC: base + TC + 3]), ["xi"], ["xi"])
                    else:
                        E("v", lambda v, base=base: v.tensor_copy(
                            out=xi_sb[:, base + 259: base + 262],
                            in_=xi_sb[:, base + 3: base + 6]), ["xi"], ["xi"])
                    cbcol = cb_sb[:, d_i * NDIB + dib: d_i * NDIB + dib + 1]
                    E("s", lambda s, dib=dib, cbcol=cbcol: s.activation(
                        out=xcb_sb[:, dib * TC:(dib + 1) * TC], in_=cacc[1][:],
                        func=AF.Silu, bias=cbcol), ["cacc1", "cb_sb"], ["xcb"])
                for kc in range(NDIB):
                    E("t", lambda t, kc=kc: t.matmul(
                        out=pC[0:96, 0:TC],
                        lhsT=xpw_sb[:, (d_i * NDIB + kc) * 96:(d_i * NDIB + kc + 1) * 96],
                        rhs=xcb_sb[:, kc * TC:(kc + 1) * TC],
                        start=(kc == 0), stop=(kc == NDIB - 1)),
                      ["xpw_sb", "xcb"], ["pC"] if kc in (0, NDIB - 1) else [])
                E("s", lambda s: s.copy(out=xdbl[:], in_=pC[0:96, 0:TC]), ["pC"], ["xdbl"])
                E("s", lambda s: s.copy(out=xdblb[:], in_=pC[0:64, 0:TC]), ["pC"], ["xdblb"])
                E("v", lambda v: v.tensor_copy(out=bcsrc[:], in_=pC[64:96, 0:TC]),
                  ["pC"], ["bcsrc"])
                E.dma("gb", bcflat[0:1, :], bcsrc[:], ["bcsrc"], ["bcflat"])
                for bi, (tgt, tk) in enumerate(((Bb_sb, "Bb"), (Cb_sb, "Cb"))):
                    for pr in range(8):
                        for q_ in range(2):
                            n = pr * 2 + q_
                            E("t", lambda t, bi=bi, n=n, q_=q_: t.matmul(
                                out=pC[:, q_ * 256: q_ * 256 + TC],
                                lhsT=ones_bf[:],
                                rhs=bcflat[0:1, (16 * bi + n) * TC:(16 * bi + n + 1) * TC],
                                start=True, stop=True),
                              ["ones", "bcflat"], ["pC"])
                            if TC > 256:
                                break
                        E("s", lambda s, tgt=tgt, pr=pr: s.copy(
                            out=tgt[:, pr * 2 * TC: pr * 2 * TC + min(2 * TC, 512)],
                            in_=pC[:, 0:min(2 * TC, 512)]), ["pC"], [tk])

            def scan_head(d, d_i, fwd, ch, dib):
                dp = dib % 2
                E("t", lambda t, dib=dib: t.matmul(
                    out=pD[:, 0:TC],
                    lhsT=dtw_sb[:, d_i * DI + dib * 128: d_i * DI + (dib + 1) * 128],
                    rhs=xdblb[:, 0:TC], start=True, stop=True),
                  ["dtw_sb", "xdblb"], ["pD"])
                dtbcol = dtb_sb[:, d_i * NDIB + dib: d_i * NDIB + dib + 1]
                E("s", lambda s, dtbcol=dtbcol: s.activation(
                    out=sp_s[:], in_=pD[:, 0:TC], func=AF.Exp, bias=dtbcol),
                  ["pD", "dtb_sb"], ["sp_s"])
                E("s", lambda s, dp=dp: s.activation(
                    out=dts[dp][:], in_=sp_s[:],
                    func=AF.Ln, bias=ko_sb[:, 1:2]), ["sp_s", "ko"], [f"dts{dp}"])

            def scan_dib(d, d_i, fwd, ch, par, dib, skip_head=False):
                off_real = 2 if fwd else 0
                off_seam = 0 if fwd else SEG - 1
                st_off = SEG - 1 if fwd else 0
                dp = dib % 2
                if not skip_head:
                    scan_head(d, d_i, fwd, ch, dib)
                E("v", lambda v, dp=dp, dib=dib: v.tensor_mul(
                    us_s[:], dts[dp][:],
                    xcb_sb[:, dib * TC:(dib + 1) * TC]), [f"dts{dp}", "xcb"], ["us"])
                yp = dib % 2
                ddcol = dd_sb[:, d_i * NDIB + dib: d_i * NDIB + dib + 1]
                E("v", lambda v, dib=dib, ddcol=ddcol, yp=yp: v.tensor_scalar(
                    out=ys_s[yp][:], in0=xcb_sb[:, dib * TC:(dib + 1) * TC],
                    scalar1=ddcol, scalar2=None, op0=OP.mult), ["xcb", "dd_sb"], [f"ys{yp}"])
                for h in range(2):
                    hp = (dib * 2 + h) % 2
                    n0 = NG * h
                    for k in range(NG):
                        n = n0 + k
                        ai = (d_i * NDIB + dib) * NST + n
                        acol = a_sb[:, ai:ai + 1]
                        E("s", lambda s, hp=hp, k=k, dp=dp, acol=acol, off_real=off_real:
                          s.activation(
                            out=dAx[hp][:, k * SEG + off_real:
                                         k * SEG + off_real + TC],
                            in_=dts[dp][:], func=AF.Exp, scale=acol),
                          [f"dts{dp}", "a_sb"], [f"dAx{hp}"])
                    u_rep = bass.AP(us_s, 0, [[TC, 128], [0, NG], [1, TC]])
                    bb_ap = bass.AP(Bb_sb, n0 * TC, [[NST * TC, 128], [TC, NG], [1, TC]])
                    ub_out = bass.AP(uBx, off_real, [[EXT, 128], [SEG, NG], [1, TC]])
                    E("v", lambda v, u_rep=u_rep, bb_ap=bb_ap, ub_out=ub_out:
                      v.tensor_tensor(out=ub_out, in0=u_rep, in1=bb_ap, op=OP.mult),
                      ["us", "Bb"], ["uBx"])
                    seam_out = bass.AP(uBx, off_seam, [[EXT, 128], [SEG, NG]])
                    E("v", lambda v, seam_out=seam_out, dib=dib, n0=n0: v.tensor_copy(
                        out=seam_out, in_=st_sb[:, dib * NST + n0: dib * NST + n0 + NG]),
                      ["st", "uBx"], ["uBx"])
                    if fwd:
                        E("v", lambda v, hp=hp: v.tensor_tensor_scan(
                            out=Hx[hp][:], data0=dAx[hp][:], data1=uBx[:],
                            initial=0.0, op0=OP.mult, op1=OP.add),
                          [f"dAx{hp}", "uBx"], ["Hx0"])
                    else:
                        E("v", lambda v, hp=hp: v.tensor_tensor_scan(
                            out=rev_ap(Hx[hp], 128, EXT),
                            data0=rev_ap(dAx[hp], 128, EXT),
                            data1=rev_ap(uBx, 128, EXT),
                            initial=0.0, op0=OP.mult, op1=OP.add),
                          [f"dAx{hp}", "uBx"], ["Hx0"])
                    st_in = bass.AP(Hx[hp], st_off, [[EXT, 128], [SEG, NG]])
                    E("v", lambda v, st_in=st_in, dib=dib, n0=n0: v.tensor_copy(
                        out=st_sb[:, dib * NST + n0: dib * NST + n0 + NG], in_=st_in),
                      ["Hx0"], ["st"])
                    h_ap = bass.AP(Hx[hp], off_real, [[EXT, 128], [SEG, NG], [1, TC]])
                    cb_ap = bass.AP(Cb_sb, n0 * TC, [[NST * TC, 128], [TC, NG], [1, TC]])
                    E("v", lambda v, h_ap=h_ap, cb_ap=cb_ap, n0=n0:
                      v.tensor_tensor(out=hcn[:, n0 * TC:(n0 + NG) * TC],
                                      in0=h_ap, in1=cb_ap, op=OP.mult),
                      ["Hx0", "Cb"], ["hcn"])
                w = NST * TC // 2
                while w >= TC:
                    E("v", lambda v, w=w: v.tensor_add(
                        hcn[:, 0:w], hcn[:, 0:w], hcn[:, w:2 * w]), ["hcn"], ["hcn"])
                    w //= 2
                E("v", lambda v, yp=yp: v.tensor_add(ys_s[yp][:], ys_s[yp][:], hcn[:, 0:TC]),
                  ["hcn", f"ys{yp}"], [f"ys{yp}"])
                E("v", lambda v, dib=dib, yp=yp: v.tensor_mul(
                    ysz_sb[:, dib * TC:(dib + 1) * TC], ys_s[yp][:],
                    szb_sb[par][:, dib * TC:(dib + 1) * TC]),
                  [f"ys{yp}", f"szb{par}"], ["ysz"])

            def outproj(d, d_i, fwd, ch, par):
                t0 = ch * TC
                if not fwd:
                    for dmb in range(8):
                        E.dma("qb", af_sb[:, dmb * TC:(dmb + 1) * TC],
                              attn_d[dmb * 128:(dmb + 1) * 128, t0:t0 + TC],
                              [f"attnd{ch}"], ["af"])
                def wf_load(dmb):
                    op_ = dmb % 3
                    E.dma("gb", wf_b[op_][:], wft_d[d][:, dmb * NDIB * 128:(dmb + 1) * NDIB * 128],
                          [], [f"wf{op_}"])
                for dmb in range(3):
                    wf_load(dmb)
                for dmb in range(8):
                    op_ = dmb % 3
                    wb, wbk = wf_b[op_], f"wf{op_}"
                    ps, pk = (pE, "pE") if dmb % 2 == 0 else (pF, "pF")
                    for dib in range(NDIB):
                        E("t", lambda t, dib=dib, ps=ps, wb=wb: t.matmul(
                            out=ps[:, 0:TC],
                            lhsT=wb[:, dib * 128:(dib + 1) * 128],
                            rhs=ysz_sb[:, dib * TC:(dib + 1) * TC],
                            start=(dib == 0), stop=(dib == NDIB - 1)),
                          [wbk, "ysz"], [pk] if dib in (0, NDIB - 1) else [])
                    if dmb + 3 < 8:
                        wf_load(dmb + 3)
                    if fwd:
                        E("s", lambda s, ps=ps: s.copy(out=aev[:], in_=ps[:, 0:TC]),
                          [pk], ["aev"])
                        E.dma("qc", attn_d[dmb * 128:(dmb + 1) * 128, t0:t0 + TC],
                              aev[:], ["aev"], [f"attnd{ch}"])
                    else:
                        pbcol = pb_sb[:, dmb:dmb + 1]
                        E("v", lambda v, ps=ps, dmb=dmb, pbcol=pbcol: v.scalar_tensor_tensor(
                            out=aev[:], in0=ps[:, 0:TC], scalar=pbcol,
                            in1=af_sb[:, dmb * TC:(dmb + 1) * TC],
                            op0=OP.add, op1=OP.add), [pk, "pb_sb", "af"], ["aev"])
                        for tt in range(2):
                            E("t", lambda t, tt=tt: t.transpose(
                                out=pT2[:, tt * 128:(tt + 1) * 128],
                                in_=aev[:, tt * 128:(tt + 1) * 128],
                                identity=id_sb[:]), ["aev", "id_sb"], ["pT2"])
                        for tt in range(2):
                            E("v", lambda v, tt=tt, dmb=dmb: v.tensor_add(
                                x_t[par][tt][:, dmb * 128:(dmb + 1) * 128],
                                x_t[par][tt][:, dmb * 128:(dmb + 1) * 128],
                                pT2[:, tt * 128:(tt + 1) * 128]),
                              ["pT2", f"x_t{par}{tt}"], [f"x_t{par}{tt}"])
                if not fwd:
                    for tt in range(2):
                        rows = slice(t0 + tt * 128, t0 + tt * 128 + 128)
                        E.dma("qc", x2_d[rows, :], x_t[par][tt][:], [f"x_t{par}{tt}"], [])

            for d_i, d in enumerate(_dirs):
                fwd = d == "f"
                E("v", lambda v: v.memset(st_sb[:], 0.0), ["st"], ["st"])
                off_pad = 1 if fwd else TC
                for dxi in range(2):
                    E("v", lambda v, dxi=dxi: v.memset(dAx[dxi][:], 0.0),
                      [f"dAx{dxi}"], [f"dAx{dxi}"])
                    pad_ap = bass.AP(dAx[dxi], off_pad, [[EXT, 128], [SEG, NG]])
                    E("v", lambda v, pad_ap=pad_ap, dxi=dxi: v.memset(pad_ap, 1.0),
                      [f"dAx{dxi}"], [f"dAx{dxi}"])
                E("v", lambda v: v.memset(uBx[:], 0.0), ["uBx"], ["uBx"])
                E("v", lambda v: v.memset(xi_sb[:], 0.0), ["xi"], ["xi"])
                chunks = list(range(_nch)) if fwd else list(range(_nch - 1, -1, -1))
                frontend(d, d_i, fwd, chunks[0], 0)
                middle(d, d_i, fwd, chunks[0])
                scan_head(d, d_i, fwd, chunks[0], 0)
                for ci, ch in enumerate(chunks):
                    par = ci % 2
                    scan_dib(d, d_i, fwd, ch, par, 0, skip_head=True)
                    fe = (frontend_pieces(d, d_i, fwd, chunks[ci + 1], 1 - par)
                          if ci + 1 < len(chunks) else [])
                    fi = 0
                    for dib in range(1, NDIB):
                        scan_dib(d, d_i, fwd, ch, par, dib)
                        take = max(0, min(len(fe) - fi,
                                          (len(fe) * dib) // (NDIB - 1) - fi))
                        for _ in range(take):
                            fe[fi]()
                            fi += 1
                    while fi < len(fe):
                        fe[fi]()
                        fi += 1
                    if ci + 1 < len(chunks):
                        middle(d, d_i, fwd, chunks[ci + 1])
                        scan_head(d, d_i, fwd, chunks[ci + 1], 0)
                    outproj(d, d_i, fwd, ch, par)
            E.barrier_all()

        if "F" not in _phases:
            with nc.sbuf_tensor("zo", [128, DM], F32) as zo:
                E("v", lambda v: v.memset(zo[:], 0.0), [], ["zo"])
                for tile in range(T // 128):
                    E.dma("qc", out_h[tile * 128:(tile + 1) * 128, :], zo[:], ["zo"], [])
            for i in range(E.DMA_STREAMS["qc"][1]):
                sub = f"qc{i}"
                if E.cnt[sub]:
                    nc.sync.wait_ge(E.sem[sub], E.cnt[sub])
            return nc
        with nc.sbuf_tensor("ternb", [128, 8 * DM], BF16) as ternb, \
             nc.sbuf_tensor("tld", [128, 8 * DM], I32) as tld, \
             nc.sbuf_tensor("tcv", [128, 4096], F32) as tcv, \
             nc.sbuf_tensor("fb_bc", [128, DM], F32) as fb_bc, \
             nc.sbuf_tensor("sc_c", [128, 2], F32) as sc_c, \
             nc.sbuf_tensor("x2_t", [128, DM], F32) as x2_t, \
             nc.sbuf_tensor("f_ln", [128, 2 * DM], F32) as f_ln, \
             nc.sbuf_tensor("f_c", [128, 8], F32) as f_c, \
             nc.sbuf_tensor("xq_tm", [128, DM], BF16) as xq_tm, \
             nc.sbuf_tensor("xq_fm", [128, DM], BF16) as xq_fm, \
             nc.sbuf_tensor("f_ep", [128, 512], F32) as f_ep, \
             nc.sbuf_tensor("ofin", [128, DM], F32) as ofin:
            E.dma("qa", tld[:], tern_h, [], ["tld"])
            for j in range(2):
                sl = slice(j * 4096, (j + 1) * 4096)
                E("v", lambda v, sl=sl: v.tensor_copy(out=tcv[:], in_=tld[:, sl]),
                  ["tld"], ["tcv"])
                E("v", lambda v, sl=sl: v.tensor_scalar_add(
                    ternb[:, sl], tcv[:], -1.0), ["tcv"], ["ternb"])
            E.dma("qa", fb_bc[0:1, :], fbias_h, [], ["fb"])
            n_bc = 1
            while n_bc < 128:
                m = min(n_bc, 128 - n_bc)
                E.dma("qa", fb_bc[n_bc:n_bc + m, :], fb_bc[0:m, :], ["fb"], ["fb"])
                n_bc += m
            E.dma("qa", sc_c[0:1, 0:1], ascale_h, [], ["sc"])
            n_bc = 1
            while n_bc < 128:
                m = min(n_bc, 128 - n_bc)
                E.dma("qa", sc_c[n_bc:n_bc + m, 0:1], sc_c[0:m, 0:1], ["sc"], ["sc"])
                n_bc += m
            E("v", lambda v: v.tensor_scalar_max(sc_c[:, 0:1], sc_c[:, 0:1], 1e-5),
              ["sc"], ["sc"])
            E("v", lambda v: v.reciprocal(out=sc_c[:, 1:2], in_=sc_c[:, 0:1]), ["sc"], ["sc"])

            for tile in range(T // 128):
                rows = slice(tile * 128, tile * 128 + 128)
                E.dma("qb", x2_t[:], x2_d[rows, :], [], ["x2t"])
                E("v", lambda v: v.reduce_sum(f_c[:, 0:1], x2_t[:], axis=mybir.AxisListType.X),
                  ["x2t"], ["f_c"])
                E("v", lambda v: v.tensor_scalar_mul(f_c[:, 1:2], f_c[:, 0:1], -1.0 / DM),
                  ["f_c"], ["f_c"])
                E("v", lambda v: v.tensor_scalar_add(f_ln[:, 0:DM], x2_t[:], f_c[:, 1:2]),
                  ["x2t", "f_c"], ["f_ln"])
                E("s", lambda s: s.activation(out=f_ln[:, DM:2 * DM], in_=f_ln[:, 0:DM],
                                              func=AF.Square, accum_out=f_c[:, 2:3]),
                  ["f_ln"], ["f_ln", "f_c"])
                E("s", lambda s: s.activation(out=f_c[:, 3:4], in_=f_c[:, 2:3], func=AF.Sqrt,
                                              scale=ko_sb[:, 2:3], bias=ko_sb[:, 0:1]),
                  ["f_c", "ko"], ["f_c"])
                E("v", lambda v: v.reciprocal(out=f_c[:, 4:5], in_=f_c[:, 3:4]), ["f_c"], ["f_c"])
                E("v", lambda v: v.tensor_scalar_mul(f_ln[:, 0:DM], f_ln[:, 0:DM], f_c[:, 4:5]),
                  ["f_ln", "f_c"], ["f_ln"])
                E("v", lambda v: v.tensor_scalar(
                    out=f_ln[:, DM:2 * DM], in0=f_ln[:, 0:DM], scalar1=sc_c[:, 1:2],
                    scalar2=MAGIC, op0=OP.mult, op1=OP.add), ["f_ln", "sc"], ["f_ln"])
                E("v", lambda v: v.tensor_scalar(
                    out=f_ln[:, 0:DM], in0=f_ln[:, DM:2 * DM], scalar1=MAGIC,
                    scalar2=127.0, op0=OP.subtract, op1=OP.min), ["f_ln"], ["f_ln"])
                E("v", lambda v: v.tensor_scalar_max(xq_tm[:], f_ln[:, 0:DM], -127.0),
                  ["f_ln"], ["xq_tm"])
                for dmb in range(8):
                    E("t", lambda t, dmb=dmb: t.transpose(
                        out=pT2[:, dmb * 128:(dmb + 1) * 128],
                        in_=xq_tm[:, dmb * 128:(dmb + 1) * 128], identity=id_sb[:]),
                      ["xq_tm", "id_sb"], ["pT2"])
                for dmb in range(8):
                    E("s", lambda s, dmb=dmb: s.copy(
                        out=xq_fm[:, dmb * 128:(dmb + 1) * 128],
                        in_=pT2[:, dmb * 128:(dmb + 1) * 128]), ["pT2"], ["xq_fm"])
                for half in range(2):
                    ps, pk = (pA, "pA") if half == 0 else (pB, "pB")
                    for kc in range(8):
                        E("t", lambda t, kc=kc, half=half, ps=ps: t.matmul(
                            out=ps[:, 0:512],
                            lhsT=xq_fm[:, kc * 128:(kc + 1) * 128],
                            rhs=ternb[:, kc * DM + half * 512: kc * DM + (half + 1) * 512],
                            start=(kc == 0), stop=(kc == 7)),
                          ["xq_fm", "ternb"], [pk] if kc in (0, 7) else [])
                    E("v", lambda v, ps=ps, half=half: v.scalar_tensor_tensor(
                        out=f_ep[:], in0=ps[:, 0:512], scalar=sc_c[:, 0:1],
                        in1=fb_bc[:, half * 512:(half + 1) * 512],
                        op0=OP.mult, op1=OP.add), [pk, "sc", "fb"], ["f_ep"])
                    E("v", lambda v, half=half: v.tensor_add(
                        ofin[:, half * 512:(half + 1) * 512], f_ep[:],
                        x2_t[:, half * 512:(half + 1) * 512]), ["f_ep", "x2t"], ["ofin"])
                E.dma("qc", out_h[rows, :], ofin[:], ["ofin"], [])
            for i in range(E.DMA_STREAMS["qc"][1]):
                sub = f"qc{i}"
                if E.cnt[sub]:
                    nc.sync.wait_ge(E.sem[sub], E.cnt[sub])

    return nc


_NC_CACHE = None


def kernel(**inputs):
    global _NC_CACHE
    B = 8
    xs = np.ascontiguousarray(np.asarray(inputs["L0_attn_x"]))

    def lay_pblocks(w):
        r, c = w.shape
        k = r // 128
        return np.ascontiguousarray(w.reshape(k, 128, c).transpose(1, 0, 2).reshape(128, k * c))

    ins = {}
    for d, p in (("f", "fwd"), ("b", "bwd")):
        inwT = np.asarray(inputs[f"L0_attn_{p}_in_W"]).T
        t = inwT.reshape(8, 128, 32, 128).transpose(1, 2, 0, 3).reshape(128, 8 * 32 * 128)
        ins[f"inw_{d}"] = np.ascontiguousarray(t)
        ins[f"outw_{d}"] = lay_pblocks(np.asarray(inputs[f"L0_attn_{p}_out_W"]))
    ins["projt"] = lay_pblocks(np.ascontiguousarray(np.asarray(inputs["L0_attn_proj_W"]).T))
    xp = np.stack([np.asarray(inputs["L0_attn_fwd_xproj_W"]).T,
                   np.asarray(inputs["L0_attn_bwd_xproj_W"]).T])
    xp = xp.reshape(2, 16, 128, 96).transpose(2, 0, 1, 3).reshape(128, 2 * 16 * 96)
    ins["xpw"] = np.ascontiguousarray(xp)
    dtw = np.stack([np.asarray(inputs["L0_attn_fwd_dt_W"]).T,
                    np.asarray(inputs["L0_attn_bwd_dt_W"]).T])
    ins["dtw"] = np.ascontiguousarray(dtw.transpose(1, 0, 2).reshape(64, 2 * 2048))

    def two_dir(name, inner):
        s = np.stack([np.asarray(inputs[f"L0_attn_fwd_{name}"]).reshape(DI, inner),
                      np.asarray(inputs[f"L0_attn_bwd_{name}"]).reshape(DI, inner)])
        s = s.reshape(2, 16, 128, inner).transpose(2, 0, 1, 3).reshape(128, 2 * 16 * inner)
        return np.ascontiguousarray(s)

    ins["cw"] = two_dir("conv_w", 4)
    ins["cb"] = two_dir("conv_b", 1)
    ins["dtb"] = two_dir("dt_b", 1)
    ins["alog"] = two_dir("A_log", NST)
    ins["dd"] = two_dir("D", 1)
    ins["pb"] = np.ascontiguousarray(np.asarray(inputs["L0_attn_proj_b"]).reshape(8, 128).T)
    ins["tern"] = lay_pblocks(np.ascontiguousarray(
        np.asarray(inputs["L0_ffn_ternary_weight"]).T))
    ins["fbias"] = np.ascontiguousarray(np.asarray(inputs["L0_ffn_bias"]).reshape(1, DM))
    ins["ascale"] = np.ascontiguousarray(
        np.asarray(inputs["L0_ffn_act_scale"], dtype=np.float32).reshape(1, 1))
    ins["ident"] = np.eye(128, dtype=ml_dtypes.bfloat16)

    if _NC_CACHE is None:
        _NC_CACHE = build_nc()
    nc = _NC_CACHE

    in_maps = []
    for b in range(B):
        m = dict(ins)
        m["x"] = np.ascontiguousarray(xs[b])
        in_maps.append(m)
    res = run_bass_kernel_spmd(nc, in_maps, core_ids=list(range(B)))
    return np.stack([np.asarray(r["out"], dtype=np.float32) for r in res.results], axis=0)
